# version 19
# baseline (speedup 1.0000x reference)
import contextlib

import numpy as np
import ml_dtypes

import concourse.bass as bass
import concourse.mybir as mybir
from concourse.bass_utils import run_bass_kernel_spmd

F32 = mybir.dt.float32
BF16 = mybir.dt.bfloat16
AF = mybir.ActivationFunctionType
ALU = mybir.AluOpType
AX = mybir.AxisListType
BFNP = ml_dtypes.bfloat16

P = 128
D = 2048
NKC = D // P
I_FF = 1408
NIC = I_FF // P
SI = 2816
NSC = SI // P
NDC = D // P
E = 8
TOWN = 2048
PRE = 256
TLOC = TOWN + PRE
NT = TLOC // P
TB = 1024
NBLK = TOWN // TB
TT = 512

SEMS = ("qw", "qs", "qc", "pe", "dv", "sc")
_DEBUG = False



def host_prep(inp):
    x = np.ascontiguousarray(inp["L0_attn_x"]).reshape(16384, D)
    c = np.ascontiguousarray
    W = inp["L0_attn_W"]
    gw = inp["L0_ffn_gate_w"]
    uw = inp["L0_ffn_up_w"]
    dw = inp["L0_ffn_down_w"]
    ebm = np.zeros((8, 8 * P), dtype=BFNP)
    for e in range(E):
        ebm[e, e * P:(e + 1) * P] = 1.0
    shared = {
        "wlam": c(W.T.reshape(NKC, P, NKC, P).transpose(2, 1, 0, 3)),
        "blam": c(inp["L0_attn_b"].reshape(NKC, P).T),
        "rt": c(inp["L0_ffn_router_w"].T.reshape(NKC, P, E)
                .transpose(1, 0, 2).reshape(P, NKC * E)),
        "gate": c(gw.reshape(E, NKC, P, NIC, P).transpose(0, 3, 2, 1, 4)),
        "up": c(uw.reshape(E, NKC, P, NIC, P).transpose(0, 3, 2, 1, 4)),
        "down": c(dw.reshape(E, NIC, P, NDC, P)),
        "sgate": c(inp["L0_ffn_s_gate_w"].reshape(NKC, P, NSC, P)
                   .transpose(2, 1, 0, 3)),
        "sup": c(inp["L0_ffn_s_up_w"].reshape(NKC, P, NSC, P)
                 .transpose(2, 1, 0, 3)),
        "sdown": c(inp["L0_ffn_s_down_w"].reshape(NSC, P, NDC, P)),
        "bg": c(inp["L0_ffn_s_gate_b"].reshape(NSC, P).T),
        "bu": c(inp["L0_ffn_s_up_b"].reshape(NSC, P).T),
        "bd": c(inp["L0_ffn_s_down_b"].reshape(NDC, P).T),
        "id128b": np.eye(P, dtype=BFNP),
        "id128f": np.eye(P, dtype=np.float32),
        "id8f": np.eye(8, dtype=np.float32),
        "ebm": ebm,
    }
    maps = []
    for core in range(8):
        own = x[core * TOWN:(core + 1) * TOWN]
        if core % 2 == 0:
            pre = np.zeros((PRE, D), np.float32)
        else:
            pre = x[core * TOWN - PRE:core * TOWN]
        m = dict(shared)
        m["x"] = c(np.concatenate([pre, own], axis=0))
        maps.append(m)
    return maps



class Builder:
    def __init__(self):
        self.ctr = {s: 0 for s in SEMS}
        self.script = {k: [] for k in ("tensor", "vector", "scalar", "gpsimd", "sync")}
        self.pool_rel = {}
        self.pool_next = 0
        self.pt_rel = {}
        self.pt_next = 0

    def emit(self, eng, build, waits=(), inc=None, inc_n=1):
        w = [(s, v) for (s, v) in waits if v and v > 0]
        val = None
        if inc is not None:
            self.ctr[inc] += inc_n
            val = self.ctr[inc]
        self.script[eng].append((w, build, inc, inc_n))
        return val

    def snap(self):
        return [(s, self.ctr[s]) for s in SEMS]

    def barrier(self):
        snap = self.snap()
        for e in ("tensor", "vector", "scalar", "gpsimd", "sync"):
            self.emit(e, None, waits=snap)

    def pool_acquire(self):
        u = self.pool_next
        self.pool_next += 1
        waits = [self.pool_rel[u - 6]] if u >= 6 else []
        return u, u % 6, waits

    def pool_release(self, u, sem, val):
        self.pool_rel[u] = (sem, val)

    def pt_acquire(self):
        u = self.pt_next
        self.pt_next += 1
        waits = [self.pt_rel[u - 2]] if u >= 2 else []
        return u, u % 2, waits

    def pt_release(self, u, sem, val):
        self.pt_rel[u] = (sem, val)


def build_program():
    nc = bass.Bass("TRN2", target_bir_lowering=False)

    dx = nc.dram_tensor("x", [TLOC, D], F32, kind="ExternalInput").ap()
    dwlam = nc.dram_tensor("wlam", [NKC, P, NKC, P], F32, kind="ExternalInput").ap()
    dblam = nc.dram_tensor("blam", [P, NKC], F32, kind="ExternalInput").ap()
    drt = nc.dram_tensor("rt", [P, NKC * E], F32, kind="ExternalInput").ap()
    dgate = nc.dram_tensor("gate", [E, NIC, P, NKC, P], F32, kind="ExternalInput").ap()
    dup = nc.dram_tensor("up", [E, NIC, P, NKC, P], F32, kind="ExternalInput").ap()
    ddown = nc.dram_tensor("down", [E, NIC, P, NDC, P], F32, kind="ExternalInput").ap()
    dsgate = nc.dram_tensor("sgate", [NSC, P, NKC, P], F32, kind="ExternalInput").ap()
    dsup = nc.dram_tensor("sup", [NSC, P, NKC, P], F32, kind="ExternalInput").ap()
    dsdown = nc.dram_tensor("sdown", [NSC, P, NDC, P], F32, kind="ExternalInput").ap()
    dbg = nc.dram_tensor("bg", [P, NSC], F32, kind="ExternalInput").ap()
    dbu = nc.dram_tensor("bu", [P, NSC], F32, kind="ExternalInput").ap()
    dbd = nc.dram_tensor("bd", [P, NDC], F32, kind="ExternalInput").ap()
    did128b = nc.dram_tensor("id128b", [P, P], BF16, kind="ExternalInput").ap()
    did128f = nc.dram_tensor("id128f", [P, P], F32, kind="ExternalInput").ap()
    did8f = nc.dram_tensor("id8f", [8, 8], F32, kind="ExternalInput").ap()
    debm = nc.dram_tensor("ebm", [8, 8 * P], BF16, kind="ExternalInput").ap()
    dstok = nc.dram_tensor("stok", [D, TOWN], BF16, kind="Internal").ap()
    dout = nc.dram_tensor("out", [TOWN, D], F32, kind="ExternalOutput").ap()
    if _DEBUG:
        ddbg_xnf = nc.dram_tensor("dbg_xnf", [P, TOWN], BF16, kind="ExternalOutput").ap()
        ddbg_stat = nc.dram_tensor("dbg_stat", [P, 16], F32, kind="ExternalOutput").ap()
        ddbg_lam = nc.dram_tensor("dbg_lam", [P, TLOC], F32, kind="ExternalOutput").ap()
        ddbg_sbf = nc.dram_tensor("dbg_sbf", [P, TOWN], BF16, kind="ExternalOutput").ap()
        ddbg_lt = nc.dram_tensor("dbg_lt", [P, P], F32, kind="ExternalOutput").ap()
        ddbg_cfb = nc.dram_tensor("dbg_cfb", [8, TOWN], BF16, kind="ExternalOutput").ap()
        ddbg_h = nc.dram_tensor("dbg_h", [P, TB], BF16, kind="ExternalOutput").ap()
        ddbg_wg = nc.dram_tensor("dbg_wg", [P, D], BF16, kind="ExternalOutput").ap()
        ddbg_sil = nc.dram_tensor("dbg_sil", [P, TT], F32, kind="ExternalOutput").ap()
        ddbg_su = nc.dram_tensor("dbg_su", [P, TT], F32, kind="ExternalOutput").ap()
        ddbg_stok = nc.dram_tensor("dbg_stok", [TOWN, D], BF16, kind="ExternalOutput").ap()
        ddbg_acc = nc.dram_tensor("dbg_acc", [P, NDC * TB], BF16, kind="ExternalOutput").ap()

    B = Builder()
    emit = B.emit

    es = contextlib.ExitStack()
    with es:
        En = es.enter_context
        xnf = En(nc.sbuf_tensor("xnf", [P, NKC * TOWN], BF16))
        cfb = En(nc.sbuf_tensor("cfb", [8, TOWN], BF16))
        ebt = En(nc.sbuf_tensor("ebt", [8, 8 * P], BF16))
        rtb = En(nc.sbuf_tensor("rtb", [P, NKC * E], BF16))
        idb = En(nc.sbuf_tensor("idb", [P, P], BF16))
        id8 = En(nc.sbuf_tensor("id8", [8, 8], F32))
        blamt = En(nc.sbuf_tensor("blamt", [P, NKC], F32))
        bgt = En(nc.sbuf_tensor("bgt", [P, NSC], F32))
        but = En(nc.sbuf_tensor("but", [P, NSC], F32))
        bdt = En(nc.sbuf_tensor("bdt", [P, NDC], F32))
        ARENA_B = 140288
        arena = En(nc.sbuf_tensor("arena", [P, ARENA_B // 4], F32))
        arena_bf = arena.bitcast(BF16)

        def cf32(off_b, cols, parts=P):
            assert off_b % 4 == 0 and off_b + 4 * cols <= ARENA_B, off_b
            return arena[0:parts, off_b // 4: off_b // 4 + cols]

        def cbf(off_b, cols, parts=P):
            assert off_b % 2 == 0 and off_b + 2 * cols <= ARENA_B, off_b
            return arena_bf[0:parts, off_b // 2: off_b // 2 + cols]

        wg = [cbf(p * 4096, D) for p in range(2)]
        wu = [cbf(8192 + p * 4096, D) for p in range(2)]
        wd = cbf(16384, NIC * D)
        hbuf = cbf(61440, NIC * TB)
        accb = cbf(83968, NDC * TB)
        silu_t = [cf32(116736 + p * 2048, TT) for p in range(2)]
        su_t = cf32(120832, TT)
        ytmp = cf32(122880, TT)
        comb_b = cbf(124928, TB)
        gts_t = [cbf(126976 + g_ * 1024, TT) for g_ in range(2)]
        stokt = [cbf(129024 + p * 1024, TT) for p in range(3)]
        xft = [cf32(132096 + p * 2048, TT) for p in range(2)]
        outt = [cf32(136192 + p * 2048, TT) for p in range(2)]
        x_t = [cf32(61440 + p * 8192, D) for p in range(2)]
        xn_tok = [cbf(81920 + p * 4096, D) for p in range(2)]
        spk = cf32(90112, 48)
        xn_pre = cbf(102656, NKC * PRE)
        wl = [cbf(116736, D), cbf(120832, D)]
        lam_t = cf32(61440, TLOC)
        cp_t = cf32(70656, TLOC)
        s_t = cf32(79872, TLOC)
        s_bf = cbf(98560, TOWN)
        lg_fm = cf32(61440, TOWN, parts=8)
        lt_all = cf32(70656, NKC * E)
        m8_t = cf32(71168, 8)
        m1_all = cf32(71232, NKC)
        m2_all = cf32(71296, NKC)
        dm_all = cf32(71360, NKC)
        w1_all = cf32(71424, NKC)
        w2_all = cf32(71488, NKC)
        buw = cf32(71552, NKC)
        auw = cf32(71616, NKC)
        d_all = cf32(71680, P)
        sgn_all = cf32(72192, P)
        t2_all = cf32(72704, P)
        ct_all = cf32(73216, P)
        ctmp_all = cf32(73728, P)
        ct_bf = cbf(74240, P)

        pool = [En(nc.psum_tensor(f"ps{i}", [P, TT], F32)) for i in range(6)]
        ptp = [En(nc.psum_tensor(f"pt{i}", [P, 1024], BF16)) for i in range(2)]
        xnf3 = xnf[:, :].rearrange("p (k t) -> p k t", k=NKC)
        xn_pre3 = xn_pre.rearrange("p (k t) -> p k t", k=NKC)
        sem = {s: En(nc.semaphore(s)) for s in SEMS}
        blk = En(nc.Block())

        for (dst, src) in [(blamt, dblam), (bgt, dbg), (but, dbu), (bdt, dbd),
                           (id8, did8f), (idb, did128b),
                           (ebt, debm)]:
            emit("sync", lambda h, d=dst, s=src: h.dma_start(out=d[:], in_=s),
                 inc="qs", inc_n=16)
        qs_consts = B.ctr["qs"]
        emit("gpsimd", lambda h: h.dma_start(out=rtb[:], in_=drt),
             inc="qw", inc_n=16)
        qw_consts = B.ctr["qw"]
        pf = []
        for ic in range(2):
            emit("gpsimd", lambda h, ic=ic: h.dma_start(
                out=wg[ic % 2][:],
                in_=dsgate[ic].rearrange("p a b -> p (a b)")), inc="qw", inc_n=16)
            pf.append(emit("gpsimd", lambda h, ic=ic: h.dma_start(
                out=wu[ic % 2][:],
                in_=dsup[ic].rearrange("p a b -> p (a b)")), inc="qw", inc_n=16))
        pf.append(emit("gpsimd", lambda h: h.dma_start(
            out=wd[:].rearrange("p (a m) -> p a m", a=NIC),
            in_=dsdown[0:NIC].rearrange("a p b c -> p a (b c)")),
            inc="qw", inc_n=16))
        prefetch0 = tuple(pf)

        xt_free = [[], []]
        xtk_free = [[], []]
        ln_xnf_ms = 0
        nm_ms = {}
        for i in range(NT):
            p = i % 2
            qs_i = emit("sync",
                        lambda h, p=p, i=i: h.dma_start(
                            out=x_t[p][:], in_=dx[i * P:(i + 1) * P, :]),
                        waits=xt_free[p], inc="qs", inc_n=16)
            ms_red = emit("vector",
                          lambda h, p=p: h.tensor_reduce(
                              out=spk[:, 0:1], in_=x_t[p][:], axis=AX.X, op=ALU.add),
                          waits=[("qs", qs_i)], inc="dv")
            emit("vector", None, waits=[("dv", ms_red)])
            ms_mean = emit("vector", lambda h: h.tensor_scalar(
                out=spk[:, 1:2], in0=spk[:, 0:1], scalar1=1.0 / D, scalar2=None,
                op0=ALU.mult), inc="dv")
            ms_sq = emit("scalar",
                         lambda h, p=p: h.activation(
                             out=xn_tok[p][:], in_=x_t[p][:], func=AF.Square,
                             accum_out=spk[:, 2:3]),
                         waits=[("qs", qs_i)], inc="sc")
            ms_t1 = emit("vector", lambda h: h.tensor_scalar(
                out=spk[:, 3:4], in0=spk[:, 2:3], scalar1=1.0 / D, scalar2=1e-5,
                op0=ALU.mult, op1=ALU.add), waits=[("sc", ms_sq)], inc="dv")
            ms_m2t = emit("vector", lambda h: h.tensor_tensor(
                out=spk[:, 4:5], in0=spk[:, 1:2], in1=spk[:, 1:2], op=ALU.mult),
                waits=[("dv", ms_mean)], inc="dv")
            ms_vp = emit("vector", lambda h: h.tensor_tensor(
                out=spk[:, 5:6], in0=spk[:, 3:4], in1=spk[:, 4:5], op=ALU.subtract),
                waits=[("dv", ms_m2t)], inc="dv")
            ms_sd = emit("scalar", lambda h: h.activation(
                out=spk[:, 6:7], in_=spk[:, 5:6], func=AF.Sqrt),
                waits=[("dv", ms_vp)], inc="sc")
            ms_rc = emit("vector", lambda h, i=i: h.reciprocal(
                out=spk[:, 8 + 2 * i:9 + 2 * i], in_=spk[:, 6:7]),
                waits=[("sc", ms_sd)], inc="dv")
            emit("vector", None, waits=[("dv", ms_rc)])
            nm_ms[i] = emit("vector", lambda h, i=i: h.scalar_tensor_tensor(
                out=spk[:, 9 + 2 * i:10 + 2 * i], in0=spk[:, 1:2], scalar=-1.0,
                in1=spk[:, 8 + 2 * i:9 + 2 * i], op0=ALU.mult, op1=ALU.mult),
                inc="dv")
            xt_free[p] = [("sc", ms_sq), ("dv", ms_red)]
        for i in range(NT):
            p = i % 2
            qs_i = emit("sync",
                        lambda h, p=p, i=i: h.dma_start(
                            out=x_t[p][:], in_=dx[i * P:(i + 1) * P, :]),
                        waits=xt_free[p], inc="qs", inc_n=16)
            ms_app = emit("scalar",
                          lambda h, p=p, i=i: h.activation(
                              out=xn_tok[p][:], in_=x_t[p][:], func=AF.Identity,
                              scale=spk[:, 8 + 2 * i:9 + 2 * i],
                              bias=spk[:, 9 + 2 * i:10 + 2 * i]),
                          waits=[("qs", qs_i), ("dv", nm_ms[i])] + xtk_free[p],
                          inc="sc")
            xt_free[p] = [("sc", ms_app)]
            last_tr = 0
            for grp in range(2):
                u, b_, w_ = B.pt_acquire()
                for j in range(8):
                    kc = grp * 8 + j
                    waits = list(w_) if j == 0 else []
                    if grp == 0 and j == 0:
                        waits += [("sc", ms_app), ("qs", qs_consts)]
                    last_tr = emit("tensor",
                                   lambda h, p=p, kc=kc, b_=b_, j=j: h.matmul(
                                       ptp[b_][:, j * P:(j + 1) * P],
                                       xn_tok[p][:, kc * P:(kc + 1) * P], idb[:],
                                       is_transpose=True,
                                       start=(j == 0), stop=(j == 7)),
                                   waits=waits,
                                   inc="pe" if j == 7 else None)
                if i < 2:
                    dst = xn_pre3[:, grp * 8:(grp + 1) * 8, i * P:(i + 1) * P]
                else:
                    dst = xnf3[:, grp * 8:(grp + 1) * 8, (i - 2) * P:(i - 1) * P]
                ln_xnf_ms = emit("vector",
                                 lambda h, dst=dst, b_=b_: h.tensor_copy(
                                     dst,
                                     ptp[b_][:].rearrange("p (a b) -> p a b", a=8)),
                                 waits=[("pe", last_tr)], inc="dv")
                B.pt_release(u, "dv", ln_xnf_ms)
            xtk_free[p] = [("pe", last_tr)]

        if _DEBUG:
            emit("sync", lambda h: h.dma_start(out=ddbg_xnf, in_=xnf[:, 0:TOWN]),
                 waits=[("dv", B.ctr["dv"])], inc="qs", inc_n=16)
            emit("sync", lambda h: h.dma_start(out=ddbg_stat, in_=spk[:]),
                 inc="qs", inc_n=16)

        for tau in range(4):
            u, b_, w_ = B.pool_acquire()
            ms = 0
            for kc in range(NKC):
                waits = list(w_) if kc == 0 else []
                if kc == 0:
                    waits += [("dv", ln_xnf_ms), ("qw", qw_consts)]
                ms = emit("tensor",
                          lambda h, kc=kc, tau=tau, b_=b_: h.matmul(
                              pool[b_][0:8, :],
                              rtb[:, kc * 8:(kc + 1) * 8],
                              xnf[:, kc * TOWN + tau * TT: kc * TOWN + (tau + 1) * TT],
                              start=(kc == 0), stop=(kc == NKC - 1)),
                          waits=waits, inc="pe" if kc == NKC - 1 else None)
            ms_ev = emit("vector",
                         lambda h, tau=tau, b_=b_: h.tensor_copy(
                             lg_fm[:, tau * TT:(tau + 1) * TT], pool[b_][0:8, :]),
                         waits=[("pe", ms)] +
                         ([("sc", B.ctr["sc"])] if tau == 0 else []), inc="dv")
            B.pool_release(u, "dv", ms_ev)
        for tc in range(NKC):
            u, b_, w_ = B.pool_acquire()
            ms = emit("tensor",
                      lambda h, tc=tc, b_=b_: h.transpose(
                          pool[b_][:, 0:8], lg_fm[0:8, tc * P:(tc + 1) * P], id8[:]),
                      waits=list(w_) + ([("dv", B.ctr["dv"]), ("qs", qs_consts)]
                                        if tc == 0 else []),
                      inc="pe")
            ms_ev = emit("vector",
                         lambda h, tc=tc, b_=b_: h.tensor_copy(
                             lt_all[:, tc * 8:(tc + 1) * 8], pool[b_][:, 0:8]),
                         waits=[("pe", ms)], inc="dv")
            B.pool_release(u, "dv", ms_ev)
        for tc in range(NKC):
            ms_mx = emit("vector", lambda h, tc=tc: h.max(
                m8_t[:], lt_all[:, tc * 8:(tc + 1) * 8]),
                waits=([("dv", B.ctr["dv"])] if tc == 0 else []), inc="dv")
            emit("vector", None, waits=[("dv", ms_mx)])
            emit("vector", lambda h, tc=tc: h.tensor_copy(
                m1_all[:, tc:tc + 1], m8_t[:, 0:1]))
            emit("vector", lambda h, tc=tc: h.tensor_copy(
                m2_all[:, tc:tc + 1], m8_t[:, 1:2]), inc="dv")
            emit("vector", lambda h, tc=tc: h.tensor_scalar(
                out=d_all[:, tc * 8:(tc + 1) * 8],
                in0=lt_all[:, tc * 8:(tc + 1) * 8],
                scalar1=m8_t[:, 1:2], scalar2=None, op0=ALU.subtract), inc="dv")
        emit("vector", None, waits=[("dv", B.ctr["dv"])])
        ms_d = emit("vector", lambda h: h.tensor_tensor(
            out=dm_all[:], in0=m1_all[:], in1=m2_all[:], op=ALU.subtract), inc="dv")
        emit("scalar", lambda h: h.activation(
            out=w1_all[:], in_=dm_all[:], func=AF.Sigmoid), waits=[("dv", ms_d)])
        emit("scalar", lambda h: h.activation(
            out=w2_all[:], in_=dm_all[:], func=AF.Sigmoid, scale=-1.0))
        ms_sg = emit("scalar", lambda h: h.activation(
            out=sgn_all[:], in_=d_all[:], func=AF.Sign), inc="sc")
        emit("vector", lambda h: h.tensor_tensor(
            out=t2_all[:], in0=sgn_all[:], in1=sgn_all[:], op=ALU.mult),
            waits=[("sc", ms_sg)])
        emit("vector", lambda h: h.tensor_scalar(
            out=buw[:], in0=w1_all[:], scalar1=0.5, scalar2=None, op0=ALU.mult))
        ms_au = emit("vector", lambda h: h.scalar_tensor_tensor(
            out=auw[:], in0=w1_all[:], scalar=0.5, in1=w2_all[:],
            op0=ALU.mult, op1=ALU.subtract), inc="dv")
        emit("vector", None, waits=[("dv", ms_au)])
        ct_ms = 0
        for tc in range(NKC):
            ct_ms = emit("vector", lambda h, tc=tc: h.tensor_scalar(
                out=ctmp_all[:, tc * 8:(tc + 1) * 8],
                in0=t2_all[:, tc * 8:(tc + 1) * 8],
                scalar1=auw[:, tc:tc + 1], scalar2=w2_all[:, tc:tc + 1],
                op0=ALU.mult, op1=ALU.add), inc="dv")
        emit("vector", None, waits=[("dv", ct_ms)])
        for tc in range(NKC):
            ct_ms = emit("vector", lambda h, tc=tc: h.scalar_tensor_tensor(
                out=ct_all[:, tc * 8:(tc + 1) * 8],
                in0=sgn_all[:, tc * 8:(tc + 1) * 8],
                scalar=buw[:, tc:tc + 1], in1=ctmp_all[:, tc * 8:(tc + 1) * 8],
                op0=ALU.mult, op1=ALU.add),
                inc="dv")
        ms_ctb = emit("vector", lambda h: h.tensor_copy(
            ct_bf[:], ct_all[:]), waits=[("dv", ct_ms)], inc="dv")
        for tc in range(NKC):
            u, b_, w_ = B.pt_acquire()
            ms = emit("tensor",
                      lambda h, tc=tc, b_=b_: h.transpose(
                          ptp[b_][0:8, 0:P], ct_bf[:, tc * 8:(tc + 1) * 8], idb[:]),
                      waits=list(w_) + ([("dv", ms_ctb)] if tc == 0 else []),
                      inc="pe")
            ms_ev = emit("vector",
                         lambda h, tc=tc, b_=b_: h.tensor_copy(
                             cfb[0:8, tc * P:(tc + 1) * P], ptp[b_][0:8, 0:P]),
                         waits=[("pe", ms)], inc="dv")
            B.pt_release(u, "dv", ms_ev)

        if _DEBUG:
            emit("sync", lambda h: h.dma_start(out=ddbg_lt, in_=lt_all[:]),
                 waits=[("dv", B.ctr["dv"])], inc="qs", inc_n=16)
            emit("sync", lambda h: h.dma_start(out=ddbg_cfb, in_=cfb[0:8, :]),
                 inc="qs", inc_n=16)
        router_done = [("sc", B.ctr["sc"]), ("dv", B.ctr["dv"])]
        gu_slot_free = [[], []]
        wl_free = [[], []]
        stq_free = [[], []]
        TAUS = [(0, PRE)] + [(PRE + j * TT, TT) for j in range(4)]
        prev_cast_ms = 0
        for ec in range(NKC):
            p = ec % 2
            qw_ec = emit("gpsimd",
                         lambda h, ec=ec, p=p: h.dma_start(
                             out=wl[p][:],
                             in_=dwlam[ec].rearrange("p a b -> p (a b)")),
                         waits=wl_free[p], inc="qw", inc_n=16)
            sig_ms = 0
            mm_last = 0
            for (c0, n) in TAUS:
                u, b_, w_ = B.pool_acquire()
                if c0 < PRE:
                    rhs_fn = (lambda kc, c0=c0, n=n:
                              xn_pre[:, kc * PRE + c0: kc * PRE + c0 + n])
                else:
                    rhs_fn = (lambda kc, c0=c0, n=n:
                              xnf[:, kc * TOWN + (c0 - PRE):
                                  kc * TOWN + (c0 - PRE) + n])
                for kc in range(NKC):
                    waits = list(w_) if kc == 0 else []
                    if kc == 0 and c0 == 0:
                        waits += [("qw", qw_ec)]
                    mm_last = emit("tensor",
                                   lambda h, kc=kc, b_=b_, rhs_fn=rhs_fn, p=p, n=n:
                                   h.matmul(
                                       pool[b_][:, 0:n],
                                       wl[p][:, kc * P:(kc + 1) * P], rhs_fn(kc),
                                       start=(kc == 0), stop=(kc == NKC - 1)),
                                   waits=waits,
                                   inc="pe" if kc == NKC - 1 else None)
                sig_ms = emit("scalar",
                              lambda h, ec=ec, b_=b_, c0=c0, n=n: h.activation(
                                  out=lam_t[:, c0:c0 + n], in_=pool[b_][:, 0:n],
                                  func=AF.Sigmoid, bias=blamt[:, ec:ec + 1]),
                              waits=[("pe", mm_last)] +
                              ([("dv", prev_cast_ms)] if c0 == 0 else []) +
                              (router_done if ec == 0 and c0 == 0 else []),
                              inc="sc")
                B.pool_release(u, "sc", sig_ms)
            wl_free[p] = [("pe", mm_last)]
            emit("vector", lambda h, ec=ec: h.scalar_tensor_tensor(
                out=cp_t[:, 0:PRE], in0=lam_t[:, 0:PRE], scalar=-1.0,
                in1=xn_pre[:, ec * PRE:(ec + 1) * PRE], op0=ALU.add, op1=ALU.mult),
                waits=[("sc", sig_ms)])
            ms_cp = emit("vector", lambda h, ec=ec: h.scalar_tensor_tensor(
                out=cp_t[:, PRE:TLOC], in0=lam_t[:, PRE:TLOC], scalar=-1.0,
                in1=xnf[:, ec * TOWN:(ec + 1) * TOWN], op0=ALU.add, op1=ALU.mult),
                inc="dv")
            emit("vector", None, waits=[("dv", ms_cp)])
            ms_scan = emit("vector", lambda h: h.tensor_tensor_scan(
                out=s_t[:], data0=lam_t[:], data1=cp_t[:], initial=0.0,
                op0=ALU.mult, op1=ALU.subtract), inc="dv")
            emit("vector", None, waits=[("dv", ms_scan)])
            ms_cast = emit("vector", lambda h: h.tensor_copy(
                s_bf[:], s_t[:, PRE:TLOC]), waits=stq_free[0], inc="dv")
            prev_cast_ms = ms_cast
            if _DEBUG and ec == 0:
                emit("sync", lambda h: h.dma_start(out=ddbg_lam, in_=lam_t[:]),
                     waits=[("dv", ms_cast)], inc="qs", inc_n=16)
                emit("sync", lambda h: h.dma_start(out=ddbg_sbf, in_=s_bf[:]),
                     inc="qs", inc_n=16)
            ms_st = emit("scalar",
                         lambda h, ec=ec: h.dma_start(
                             out=dstok[ec * P:(ec + 1) * P, :], in_=s_bf[:]),
                         waits=[("dv", ms_cast)], inc="qc", inc_n=16)
            stq_free[0] = [("qc", ms_st)]
        qc_scan = B.ctr["qc"]

        B.barrier()
        giu = [0]
        h_free = [("pe", 0)]
        acc_init_ms = {}
        wd_gate = [B.snap()]
        silu_free = [[], []]
        su_free = []
        acc_ms_last = {}
        out_free = [[], []]
        sxf_free = [[], [], []]
        xf_free = [[], []]
        gts_free = [[], []]

        def emit_pass(bi, kind, e=None, half=None, prefetch=None,
                      ic_hook=None, down_hook=None):
            nonlocal su_free
            col0 = bi * TB
            if kind == "shared":
                wsg = lambda ic: dsgate[half * NIC + ic]
                wsu = lambda ic: dsup[half * NIC + ic]
                wsd = lambda ic: dsdown[half * NIC + ic]
                bgc = lambda ic: bgt[:, half * NIC + ic: half * NIC + ic + 1]
                buc = lambda ic: but[:, half * NIC + ic: half * NIC + ic + 1]
            else:
                wsg = lambda ic: dgate[e, ic]
                wsu = lambda ic: dup[e, ic]
                wsd = lambda ic: ddown[e, ic]

            comb_ms = None
            if kind == "expert":
                for tau in range(2):
                    u, b_, w_ = B.pool_acquire()
                    ms = emit("tensor",
                              lambda h, e=e, tau=tau, b_=b_: h.matmul(
                                  pool[b_][:],
                                  ebt[0:8, e * P:(e + 1) * P],
                                  cfb[0:8, col0 + tau * TT: col0 + (tau + 1) * TT],
                                  start=True, stop=True),
                              waits=list(w_), inc="pe")
                    comb_ms = emit("vector",
                                   lambda h, tau=tau, b_=b_: h.tensor_copy(
                                       comb_b[:, tau * TT:(tau + 1) * TT],
                                       pool[b_][:]),
                                   waits=[("pe", ms)], inc="dv")
                    B.pool_release(u, "dv", comb_ms)

            my_giu = [giu[0] + k for k in range(NIC)]
            giu[0] += NIC
            gu_qw = {}

            def emit_gu_dma(ic):
                pp = my_giu[ic] % 2
                emit("gpsimd",
                     lambda h, ic=ic, pp=pp: h.dma_start(
                         out=wg[pp][:], in_=wsg(ic).rearrange("p a b -> p (a b)")),
                     waits=list(gu_slot_free[pp]), inc="qw", inc_n=16)
                gu_qw[ic] = emit("gpsimd",
                                 lambda h, ic=ic, pp=pp: h.dma_start(
                                     out=wu[pp][:],
                                     in_=wsu(ic).rearrange("p a b -> p (a b)")),
                                 inc="qw", inc_n=16)

            if kind == "shared":
                wsd_all = dsdown[half * NIC:(half + 1) * NIC]
            else:
                wsd_all = ddown[e]

            wd_qw = [0]

            def emit_wd_all():
                wd_qw[0] = emit("gpsimd",
                                lambda h: h.dma_start(
                                    out=wd[:].rearrange("p (a m) -> p a m", a=NIC),
                                    in_=wsd_all.rearrange("a p b c -> p a (b c)")),
                                waits=wd_gate[0], inc="qw", inc_n=16)

            if prefetch is not None:
                gu_qw[0], gu_qw[1], wd_qw[0] = prefetch
            else:
                emit_gu_dma(0)
                emit_gu_dma(1)
                emit_wd_all()

            h_ready = 0
            h_ms = {}
            for ic in range(NIC):
                pp = my_giu[ic] % 2
                slot_last = 0
                si_ms = {}
                hu_ps = {}
                for mat in range(2):
                    wbuf = wg[pp] if mat == 0 else wu[pp]
                    for tau in range(2):
                        u, b_, w_ = B.pool_acquire()
                        ms = 0
                        for kc in range(NKC):
                            waits = list(w_) if kc == 0 else []
                            if kc == 0 and mat == 0 and tau == 0:
                                waits += [("qw", gu_qw[ic])]
                            ms = emit("tensor",
                                      lambda h, kc=kc, b_=b_, wbuf=wbuf, tau=tau:
                                      h.matmul(
                                          pool[b_][:],
                                          wbuf[:, kc * P:(kc + 1) * P],
                                          xnf[:, kc * TOWN + col0 + tau * TT:
                                              kc * TOWN + col0 + (tau + 1) * TT],
                                          start=(kc == 0), stop=(kc == NKC - 1)),
                                      waits=waits,
                                      inc="pe" if kc == NKC - 1 else None)
                        slot_last = ms
                        if mat == 0:
                            kw = dict(bias=bgc(ic)) if kind == "shared" else {}
                            ms_si = emit("scalar",
                                         lambda h, b_=b_, tau=tau, kw=kw:
                                         h.activation(
                                             out=silu_t[tau][:], in_=pool[b_][:],
                                             func=AF.Silu, **kw),
                                         waits=[("pe", ms)] + silu_free[tau],
                                         inc="sc")
                            B.pool_release(u, "sc", ms_si)
                            si_ms[tau] = ms_si
                        else:
                            hu_ps[tau] = (u, b_, ms)
                for tau in range(2):
                    (u, b_, ms_mm) = hu_ps[tau]
                    ms_si = si_ms[tau]
                    hdst = hbuf[:, ic * TB + tau * TT: ic * TB + (tau + 1) * TT]
                    if kind == "shared":
                        ms_su = emit("scalar",
                                     lambda h, b_=b_, ic=ic: h.activation(
                                         out=su_t[:], in_=pool[b_][:],
                                         func=AF.Identity, bias=buc(ic)),
                                     waits=[("pe", ms_mm)] + su_free, inc="sc")
                        B.pool_release(u, "sc", ms_su)
                        waits = [("sc", ms_su), ("sc", ms_si)]
                        if ic == 0 and tau == 0:
                            waits += h_free
                        ms_h = emit("vector",
                                    lambda h, hdst=hdst, tau=tau: h.tensor_tensor(
                                        out=hdst, in0=silu_t[tau][:], in1=su_t[:],
                                        op=ALU.mult),
                                    waits=waits, inc="dv")
                        su_free = [("dv", ms_h)]
                    else:
                        waits = [("sc", ms_si), ("pe", ms_mm)] + su_free
                        if ic == 0 and tau == 0:
                            waits += h_free + [("dv", comb_ms)]
                        ms_m = emit("vector",
                                    lambda h, b_=b_, tau=tau: h.tensor_tensor(
                                        out=su_t[:], in0=silu_t[tau][:],
                                        in1=pool[b_][:], op=ALU.mult),
                                    waits=waits, inc="dv")
                        B.pool_release(u, "dv", ms_m)
                        ms_h = emit("vector",
                                    lambda h, hdst=hdst, tau=tau: h.tensor_tensor(
                                        out=hdst, in0=su_t[:],
                                        in1=comb_b[:, tau * TT:(tau + 1) * TT],
                                        op=ALU.mult),
                                    waits=[("dv", ms_m)], inc="dv")
                        su_free = [("dv", ms_h)]
                    silu_free[tau] = [("dv", ms_h)]
                    h_ready = ms_h
                    h_ms[ic] = ms_h
                    if (_DEBUG and bi == 0 and kind == "shared" and half == 0
                            and ic == 0 and tau == 0):
                        emit("sync", lambda h, pp=pp: h.dma_start(
                            out=ddbg_wg, in_=wg[pp][:]),
                            waits=[("dv", ms_h)], inc="qs", inc_n=16)
                        emit("sync", lambda h: h.dma_start(
                            out=ddbg_sil, in_=silu_t[0][:]), inc="qs", inc_n=16)
                        emit("sync", lambda h: h.dma_start(
                            out=ddbg_su, in_=su_t[:]), inc="qs", inc_n=16)
                        B.barrier()
                gu_slot_free[pp] = [("pe", slot_last)]
                if ic + 2 < NIC:
                    emit_gu_dma(ic + 2)
                if ic_hook is not None:
                    ic_hook(ic)

            down_last = 0
            ms_ac = 0
            for dc in range(NDC):
                for tau in range(2):
                    u, b_, w_ = B.pool_acquire()
                    ms = 0
                    for j in range(NIC):
                        waits = list(w_) if j == 0 else []
                        if dc == 0 and tau == 0:
                            waits += [("dv", h_ms[j])]
                            if j == 0:
                                waits += [("qw", wd_qw[0])]
                        ms = emit("tensor",
                                  lambda h, j=j, dc=dc, tau=tau, b_=b_: h.matmul(
                                      pool[b_][:],
                                      wd[:, j * D + dc * P: j * D + (dc + 1) * P],
                                      hbuf[:, j * TB + tau * TT:
                                           j * TB + (tau + 1) * TT],
                                      start=(j == 0), stop=(j == NIC - 1)),
                                  waits=waits,
                                  inc="pe" if j == NIC - 1 else None)
                    down_last = ms
                    aslice = accb[:, dc * TB + tau * TT: dc * TB + (tau + 1) * TT]
                    if kind == "shared" and half == 0:
                        ms_ac = emit("scalar",
                                     lambda h, b_=b_, aslice=aslice, dc=dc:
                                     h.activation(
                                         out=aslice, in_=pool[b_][:],
                                         func=AF.Identity, bias=bdt[:, dc:dc + 1]),
                                     waits=[("pe", ms)], inc="sc")
                        acc_init_ms[(dc, tau)] = ms_ac
                        B.pool_release(u, "sc", ms_ac)
                    else:
                        waits = [("pe", ms)]
                        if kind == "shared" and half == 1:
                            waits += [("sc", acc_init_ms[(dc, tau)])]
                        ms_ac = emit("vector",
                                     lambda h, b_=b_, aslice=aslice: h.tensor_tensor(
                                         out=aslice, in0=aslice, in1=pool[b_][:],
                                         op=ALU.add),
                                     waits=waits, inc="dv")
                        B.pool_release(u, "dv", ms_ac)
                    acc_ms_last[(dc, tau)] = ms_ac
                if down_hook is not None:
                    down_hook(dc)
            h_free[:] = [("pe", down_last)]
            wd_gate[0] = [("pe", down_last)]

        acc3 = accb.rearrange("p (k t) -> p k t", k=NDC)

        fin_seq = [0]

        def emit_final_piece(bi, tc, ds, acc_dep):
            tca = bi * (TB // P) + tc
            pc = fin_seq[0]
            fin_seq[0] += 1
            sp = pc % 2
            s4 = pc % 3
            gp = pc % 2
            ms_ld_s = emit("sync",
                           lambda h, tca=tca, ds=ds, s4=s4:
                           h.dma_start_transpose(
                               stokt[s4][:],
                               dstok[ds * TT:(ds + 1) * TT,
                                     tca * P:(tca + 1) * P]),
                           waits=list(sxf_free[s4]) +
                           ([("qc", qc_scan)] if fin_seq[0] == 1 else []),
                           inc="qs", inc_n=16)
            ms_ld_x = emit("sync",
                           lambda h, tca=tca, ds=ds, sp=sp: h.dma_start(
                               out=xft[sp][:],
                               in_=dx[PRE + tca * P: PRE + (tca + 1) * P,
                                      ds * TT:(ds + 1) * TT]),
                           waits=xf_free[sp], inc="qs", inc_n=16)
            u, b_, w_ = B.pt_acquire()
            ms_tr = 0
            for j in range(4):
                dc = ds * 4 + j
                ms_tr = emit("tensor",
                             lambda h, b_=b_, j=j, dc=dc, tc=tc: h.matmul(
                                 ptp[b_][:, j * P:(j + 1) * P],
                                 acc3[:, dc, tc * P:(tc + 1) * P],
                                 idb[:], is_transpose=True,
                                 start=(j == 0), stop=(j == 3)),
                             waits=(list(w_) + list(acc_dep)) if j == 0 else [],
                             inc="pe" if j == 3 else None)
            ms_sil = emit("scalar",
                          lambda h, gp=gp, b_=b_: h.activation(
                              out=gts_t[gp][:], in_=ptp[b_][:, 0:TT],
                              func=AF.Silu),
                          waits=[("pe", ms_tr)] + gts_free[gp], inc="sc")
            B.pt_release(u, "sc", ms_sil)
            ms_y = emit("vector",
                        lambda h, sp=sp, s4=s4, gp=gp: h.tensor_tensor(
                            out=ytmp[:], in0=stokt[s4][:],
                            in1=gts_t[gp][:], op=ALU.mult),
                        waits=[("qs", ms_ld_s), ("sc", ms_sil)], inc="dv")
            sxf_free[s4] = [("dv", ms_y)]
            gts_free[gp] = [("dv", ms_y)]
            ms_o = emit("vector",
                        lambda h, sp=sp: h.tensor_tensor(
                            out=outt[sp][:], in0=ytmp[:], in1=xft[sp][:],
                            op=ALU.add),
                        waits=[("qs", ms_ld_x), ("dv", ms_y)] + out_free[sp],
                        inc="dv")
            xf_free[sp] = [("dv", ms_o)]
            ms_st = emit("scalar",
                         lambda h, tca=tca, ds=ds, sp=sp: h.dma_start(
                             out=dout[tca * P:(tca + 1) * P,
                                      ds * TT:(ds + 1) * TT],
                             in_=outt[sp][:]),
                         waits=[("dv", ms_o)], inc="qc", inc_n=16)
            out_free[sp] = [("qc", ms_st)]

        def make_ic_hook(bi_prev, acc_done):
            state = [0]

            def hook(ic):
                if ic >= 8:
                    return
                for _ in range(8):
                    if state[0] < 4 * (TB // P):
                        emit_final_piece(bi_prev, state[0] // 4, state[0] % 4,
                                         [("dv", acc_done)])
                        state[0] += 1
            return hook

        acc_done_v = [0] * NBLK
        for bi in range(NBLK):
            hook = (make_ic_hook(bi - 1, acc_done_v[bi - 1]) if bi > 0 else None)
            emit_pass(bi, "shared", half=0,
                      prefetch=(prefetch0 if bi == 0 else None),
                      ic_hook=hook)
            if _DEBUG and bi == 0:
                emit("sync", lambda h: h.dma_start(out=ddbg_h, in_=hbuf[:, 0:TB]),
                     waits=[("dv", B.ctr["dv"]), ("pe", B.ctr["pe"])],
                     inc="qs", inc_n=16)
                B.barrier()
            emit_pass(bi, "shared", half=1)
            dh = None
            if bi == NBLK - 1:
                fin_state = [0]

                def dh(dc, fin_state=fin_state):
                    avail = (dc - 3) // 4
                    if avail < 0:
                        return
                    budget = 64 if dc == NDC - 1 else 6
                    while budget > 0 and fin_state[0] < 4 * (TB // P):
                        pc = fin_state[0]
                        ds, tc = pc // (TB // P), pc % (TB // P)
                        if ds > avail:
                            break
                        dep = [("dv", max(acc_ms_last[(c, t)]
                                          for c in range(ds * 4, ds * 4 + 4)
                                          for t in range(2)))]
                        emit_final_piece(NBLK - 1, tc, ds, dep)
                        fin_state[0] += 1
                        budget -= 1
            for e in range(E):
                emit_pass(bi, "expert", e=e,
                          down_hook=(dh if e == E - 1 else None))
            acc_done_v[bi] = B.ctr["dv"]
            if _DEBUG and bi == 0:
                emit("sync", lambda h: h.dma_start(out=ddbg_acc, in_=accb[:]),
                     waits=[("dv", acc_done_v[0])], inc="qs", inc_n=16)
                B.barrier()

        emit("sync", None, waits=[("qs", B.ctr["qs"]), ("qc", B.ctr["qc"])])
        emit("scalar", None, waits=[("qc", B.ctr["qc"])])

        def make_runner(entries):
            def run(h):
                for (w, build, inc, inc_n) in entries:
                    for (sname, v) in w:
                        h.wait_ge(sem[sname], v)
                    if build is not None:
                        ins = build(h)
                        if inc is not None:
                            ins.then_inc(sem[inc], inc_n)
            return run

        blk.gpsimd(make_runner(B.script["gpsimd"]))
        blk.sync(make_runner(B.script["sync"]))
        blk.scalar(make_runner(B.script["scalar"]))
        blk.vector(make_runner(B.script["vector"]))
        blk.tensor(make_runner(B.script["tensor"]))

    return nc


_NC_CACHE = []


def kernel(**inputs):
    maps = host_prep(inputs)
    if not _NC_CACHE:
        _NC_CACHE.append(build_program())
    nc = _NC_CACHE[0]
    res = run_bass_kernel_spmd(nc, maps, core_ids=list(range(8)))
    out = np.concatenate([res.results[c]["out"] for c in range(8)], axis=0)
    return np.ascontiguousarray(out.reshape(4, 4096, 2048))


# version 20 (final)
# speedup vs baseline: 1.0669x; 1.0669x over previous
import contextlib

import numpy as np
import ml_dtypes

import concourse.bass as bass
import concourse.mybir as mybir
from concourse.bass_utils import run_bass_kernel_spmd

F32 = mybir.dt.float32
BF16 = mybir.dt.bfloat16
AF = mybir.ActivationFunctionType
ALU = mybir.AluOpType
AX = mybir.AxisListType
BFNP = ml_dtypes.bfloat16

P = 128
D = 2048
NKC = D // P
I_FF = 1408
NIC = I_FF // P
SI = 2816
NSC = SI // P
NDC = D // P
E = 8
TOWN = 2048
PRE = 256
TLOC = TOWN + PRE
NT = TLOC // P
TB = 1024
NBLK = TOWN // TB
TT = 512

SEMS = ("qw", "qs", "qc", "pe", "dv", "sc")
_DEBUG = False



def host_prep(inp):
    x = np.ascontiguousarray(inp["L0_attn_x"]).reshape(16384, D)
    c = np.ascontiguousarray
    W = inp["L0_attn_W"]
    gw = inp["L0_ffn_gate_w"]
    uw = inp["L0_ffn_up_w"]
    dw = inp["L0_ffn_down_w"]
    ebm = np.zeros((8, 8 * P), dtype=BFNP)
    for e in range(E):
        ebm[e, e * P:(e + 1) * P] = 1.0
    shared = {
        "wlam": c(W.T.reshape(NKC, P, NKC, P).transpose(2, 1, 0, 3)),
        "blam": c(inp["L0_attn_b"].reshape(NKC, P).T),
        "rt": c(inp["L0_ffn_router_w"].T.reshape(NKC, P, E)
                .transpose(1, 0, 2).reshape(P, NKC * E)),
        "gate": c(gw.reshape(E, NKC, P, NIC, P).transpose(0, 3, 2, 1, 4)),
        "up": c(uw.reshape(E, NKC, P, NIC, P).transpose(0, 3, 2, 1, 4)),
        "down": c(dw.reshape(E, NIC, P, NDC, P)),
        "sgate": c(inp["L0_ffn_s_gate_w"].reshape(NKC, P, NSC, P)
                   .transpose(2, 1, 0, 3)),
        "sup": c(inp["L0_ffn_s_up_w"].reshape(NKC, P, NSC, P)
                 .transpose(2, 1, 0, 3)),
        "sdown": c(inp["L0_ffn_s_down_w"].reshape(NSC, P, NDC, P)),
        "bg": c(inp["L0_ffn_s_gate_b"].reshape(NSC, P).T),
        "bu": c(inp["L0_ffn_s_up_b"].reshape(NSC, P).T),
        "bd": c(inp["L0_ffn_s_down_b"].reshape(NDC, P).T),
        "id128b": np.eye(P, dtype=BFNP),
        "id128f": np.eye(P, dtype=np.float32),
        "id8f": np.eye(8, dtype=np.float32),
        "ebm": ebm,
    }
    maps = []
    for core in range(8):
        own = x[core * TOWN:(core + 1) * TOWN]
        if core % 2 == 0:
            pre = np.zeros((PRE, D), np.float32)
        else:
            pre = x[core * TOWN - PRE:core * TOWN]
        m = dict(shared)
        m["x"] = c(np.concatenate([pre, own], axis=0))
        maps.append(m)
    return maps



class Builder:
    def __init__(self):
        self.ctr = {s: 0 for s in SEMS}
        self.script = {k: [] for k in ("tensor", "vector", "scalar", "gpsimd", "sync")}
        self.pool_rel = {}
        self.pool_next = 0
        self.pt_rel = {}
        self.pt_next = 0

    def emit(self, eng, build, waits=(), inc=None, inc_n=1):
        w = [(s, v) for (s, v) in waits if v and v > 0]
        val = None
        if inc is not None:
            self.ctr[inc] += inc_n
            val = self.ctr[inc]
        self.script[eng].append((w, build, inc, inc_n))
        return val

    def snap(self):
        return [(s, self.ctr[s]) for s in SEMS]

    def barrier(self):
        snap = self.snap()
        for e in ("tensor", "vector", "scalar", "gpsimd", "sync"):
            self.emit(e, None, waits=snap)

    def pool_acquire(self):
        u = self.pool_next
        self.pool_next += 1
        waits = [self.pool_rel[u - 6]] if u >= 6 else []
        return u, u % 6, waits

    def pool_release(self, u, sem, val):
        self.pool_rel[u] = (sem, val)

    def pt_acquire(self):
        u = self.pt_next
        self.pt_next += 1
        waits = [self.pt_rel[u - 2]] if u >= 2 else []
        return u, u % 2, waits

    def pt_release(self, u, sem, val):
        self.pt_rel[u] = (sem, val)


def build_program():
    nc = bass.Bass("TRN2", target_bir_lowering=False)

    dx = nc.dram_tensor("x", [TLOC, D], F32, kind="ExternalInput").ap()
    dwlam = nc.dram_tensor("wlam", [NKC, P, NKC, P], F32, kind="ExternalInput").ap()
    dblam = nc.dram_tensor("blam", [P, NKC], F32, kind="ExternalInput").ap()
    drt = nc.dram_tensor("rt", [P, NKC * E], F32, kind="ExternalInput").ap()
    dgate = nc.dram_tensor("gate", [E, NIC, P, NKC, P], F32, kind="ExternalInput").ap()
    dup = nc.dram_tensor("up", [E, NIC, P, NKC, P], F32, kind="ExternalInput").ap()
    ddown = nc.dram_tensor("down", [E, NIC, P, NDC, P], F32, kind="ExternalInput").ap()
    dsgate = nc.dram_tensor("sgate", [NSC, P, NKC, P], F32, kind="ExternalInput").ap()
    dsup = nc.dram_tensor("sup", [NSC, P, NKC, P], F32, kind="ExternalInput").ap()
    dsdown = nc.dram_tensor("sdown", [NSC, P, NDC, P], F32, kind="ExternalInput").ap()
    dbg = nc.dram_tensor("bg", [P, NSC], F32, kind="ExternalInput").ap()
    dbu = nc.dram_tensor("bu", [P, NSC], F32, kind="ExternalInput").ap()
    dbd = nc.dram_tensor("bd", [P, NDC], F32, kind="ExternalInput").ap()
    did128b = nc.dram_tensor("id128b", [P, P], BF16, kind="ExternalInput").ap()
    did128f = nc.dram_tensor("id128f", [P, P], F32, kind="ExternalInput").ap()
    did8f = nc.dram_tensor("id8f", [8, 8], F32, kind="ExternalInput").ap()
    debm = nc.dram_tensor("ebm", [8, 8 * P], BF16, kind="ExternalInput").ap()
    dstok = nc.dram_tensor("stok", [D, TOWN], BF16, kind="Internal").ap()
    dout = nc.dram_tensor("out", [TOWN, D], F32, kind="ExternalOutput").ap()
    if _DEBUG:
        ddbg_xnf = nc.dram_tensor("dbg_xnf", [P, TOWN], BF16, kind="ExternalOutput").ap()
        ddbg_stat = nc.dram_tensor("dbg_stat", [P, 16], F32, kind="ExternalOutput").ap()
        ddbg_lam = nc.dram_tensor("dbg_lam", [P, TLOC], F32, kind="ExternalOutput").ap()
        ddbg_sbf = nc.dram_tensor("dbg_sbf", [P, TOWN], BF16, kind="ExternalOutput").ap()
        ddbg_lt = nc.dram_tensor("dbg_lt", [P, P], F32, kind="ExternalOutput").ap()
        ddbg_cfb = nc.dram_tensor("dbg_cfb", [8, TOWN], BF16, kind="ExternalOutput").ap()
        ddbg_h = nc.dram_tensor("dbg_h", [P, TB], BF16, kind="ExternalOutput").ap()
        ddbg_wg = nc.dram_tensor("dbg_wg", [P, D], BF16, kind="ExternalOutput").ap()
        ddbg_sil = nc.dram_tensor("dbg_sil", [P, TT], F32, kind="ExternalOutput").ap()
        ddbg_su = nc.dram_tensor("dbg_su", [P, TT], F32, kind="ExternalOutput").ap()
        ddbg_stok = nc.dram_tensor("dbg_stok", [TOWN, D], BF16, kind="ExternalOutput").ap()
        ddbg_acc = nc.dram_tensor("dbg_acc", [P, NDC * TB], BF16, kind="ExternalOutput").ap()

    B = Builder()
    emit = B.emit

    es = contextlib.ExitStack()
    with es:
        En = es.enter_context
        xnf = En(nc.sbuf_tensor("xnf", [P, NKC * TOWN], BF16))
        cfb = En(nc.sbuf_tensor("cfb", [8, TOWN], BF16))
        ebt = En(nc.sbuf_tensor("ebt", [8, 8 * P], BF16))
        rtb = En(nc.sbuf_tensor("rtb", [P, NKC * E], BF16))
        idb = En(nc.sbuf_tensor("idb", [P, P], BF16))
        id8 = En(nc.sbuf_tensor("id8", [8, 8], F32))
        blamt = En(nc.sbuf_tensor("blamt", [P, NKC], F32))
        bgt = En(nc.sbuf_tensor("bgt", [P, NSC], F32))
        but = En(nc.sbuf_tensor("but", [P, NSC], F32))
        bdt = En(nc.sbuf_tensor("bdt", [P, NDC], F32))
        ARENA_B = 140288
        arena = En(nc.sbuf_tensor("arena", [P, ARENA_B // 4], F32))
        arena_bf = arena.bitcast(BF16)

        def cf32(off_b, cols, parts=P):
            assert off_b % 4 == 0 and off_b + 4 * cols <= ARENA_B, off_b
            return arena[0:parts, off_b // 4: off_b // 4 + cols]

        def cbf(off_b, cols, parts=P):
            assert off_b % 2 == 0 and off_b + 2 * cols <= ARENA_B, off_b
            return arena_bf[0:parts, off_b // 2: off_b // 2 + cols]

        wg = [cbf(p * 4096, D) for p in range(2)]
        wu = [cbf(8192 + p * 4096, D) for p in range(2)]
        wd = cbf(16384, NIC * D)
        hbuf = cbf(61440, NIC * TB)
        accb = cbf(83968, NDC * TB)
        silu_t = [cf32(116736 + p * 2048, TT) for p in range(2)]
        su_t = cf32(120832, TT)
        ytmp = cf32(122880, TT)
        comb_b = cbf(124928, TB)
        gts_t = [cbf(126976 + g_ * 1024, TT) for g_ in range(2)]
        stokt = [cbf(129024 + p * 1024, TT) for p in range(3)]
        xft = [cf32(132096 + p * 2048, TT) for p in range(2)]
        outt = [cf32(136192 + p * 2048, TT) for p in range(2)]
        x_t = [cf32(61440 + p * 8192, D) for p in range(2)]
        xn_tok = [cbf(81920 + p * 4096, D) for p in range(2)]
        spk = cf32(90112, 48)
        xn_pre = cbf(102656, NKC * PRE)
        wl = [cbf(116736, D), cbf(120832, D)]
        lam_t = cf32(61440, TLOC)
        cp_t = cf32(70656, TLOC)
        s_t = cf32(79872, TLOC)
        s_bf = cbf(98560, TOWN)
        lg_fm = cf32(61440, TOWN, parts=8)
        lt_all = cf32(70656, NKC * E)
        m8_t = cf32(71168, 8)
        m1_all = cf32(71232, NKC)
        m2_all = cf32(71296, NKC)
        dm_all = cf32(71360, NKC)
        w1_all = cf32(71424, NKC)
        w2_all = cf32(71488, NKC)
        buw = cf32(71552, NKC)
        auw = cf32(71616, NKC)
        d_all = cf32(71680, P)
        sgn_all = cf32(72192, P)
        t2_all = cf32(72704, P)
        ct_all = cf32(129024, P)
        ctmp_all = cf32(73728, P)
        ct_bf = cbf(129536, P)

        pool = [En(nc.psum_tensor(f"ps{i}", [P, TT], F32)) for i in range(6)]
        ptp = [En(nc.psum_tensor(f"pt{i}", [P, 1024], BF16)) for i in range(2)]
        xnf3 = xnf[:, :].rearrange("p (k t) -> p k t", k=NKC)
        xn_pre3 = xn_pre.rearrange("p (k t) -> p k t", k=NKC)
        sem = {s: En(nc.semaphore(s)) for s in SEMS}
        blk = En(nc.Block())

        for (dst, src) in [(blamt, dblam), (bgt, dbg), (but, dbu), (bdt, dbd),
                           (id8, did8f), (idb, did128b),
                           (ebt, debm)]:
            emit("sync", lambda h, d=dst, s=src: h.dma_start(out=d[:], in_=s),
                 inc="qs", inc_n=16)
        qs_consts = B.ctr["qs"]
        emit("gpsimd", lambda h: h.dma_start(out=rtb[:], in_=drt),
             inc="qw", inc_n=16)
        qw_consts = B.ctr["qw"]
        pf = []
        for ic in range(2):
            emit("gpsimd", lambda h, ic=ic: h.dma_start(
                out=wg[ic % 2][:],
                in_=dsgate[ic].rearrange("p a b -> p (a b)")), inc="qw", inc_n=16)
            pf.append(emit("gpsimd", lambda h, ic=ic: h.dma_start(
                out=wu[ic % 2][:],
                in_=dsup[ic].rearrange("p a b -> p (a b)")), inc="qw", inc_n=16))
        pf.append(emit("gpsimd", lambda h: h.dma_start(
            out=wd[:].rearrange("p (a m) -> p a m", a=NIC),
            in_=dsdown[0:NIC].rearrange("a p b c -> p a (b c)")),
            inc="qw", inc_n=16))
        prefetch0 = tuple(pf)

        xt_free = [[], []]
        xtk_free = [[], []]
        ln_xnf_ms = 0
        for i in range(NT):
            p = i % 2
            qs_i = emit("sync",
                        lambda h, p=p, i=i: h.dma_start(
                            out=x_t[p][:], in_=dx[i * P:(i + 1) * P, :]),
                        waits=xt_free[p], inc="qs", inc_n=16)
            ms_red = emit("vector",
                          lambda h, p=p: h.tensor_reduce(
                              out=spk[:, 0:1], in_=x_t[p][:], axis=AX.X, op=ALU.add),
                          waits=[("qs", qs_i)], inc="dv")
            emit("vector", None, waits=[("dv", ms_red)])
            ms_mean = emit("vector", lambda h: h.tensor_scalar(
                out=spk[:, 1:2], in0=spk[:, 0:1], scalar1=1.0 / D, scalar2=None,
                op0=ALU.mult), inc="dv")
            ms_sq = emit("scalar",
                         lambda h, p=p: h.activation(
                             out=xn_tok[p][:], in_=x_t[p][:], func=AF.Square,
                             accum_out=spk[:, 2:3]),
                         waits=[("qs", qs_i)] + xtk_free[p], inc="sc")
            ms_t1 = emit("vector", lambda h: h.tensor_scalar(
                out=spk[:, 3:4], in0=spk[:, 2:3], scalar1=1.0 / D, scalar2=1e-5,
                op0=ALU.mult, op1=ALU.add), waits=[("sc", ms_sq)], inc="dv")
            ms_m2t = emit("vector", lambda h: h.tensor_tensor(
                out=spk[:, 4:5], in0=spk[:, 1:2], in1=spk[:, 1:2], op=ALU.mult),
                waits=[("dv", ms_mean)], inc="dv")
            ms_vp = emit("vector", lambda h: h.tensor_tensor(
                out=spk[:, 5:6], in0=spk[:, 3:4], in1=spk[:, 4:5], op=ALU.subtract),
                waits=[("dv", ms_m2t)], inc="dv")
            ms_sd = emit("scalar", lambda h: h.activation(
                out=spk[:, 6:7], in_=spk[:, 5:6], func=AF.Sqrt),
                waits=[("dv", ms_vp)], inc="sc")
            ms_rc = emit("vector", lambda h: h.reciprocal(
                out=spk[:, 7:8], in_=spk[:, 6:7]),
                waits=[("sc", ms_sd)], inc="dv")
            emit("vector", None, waits=[("dv", ms_rc)])
            ms_nm = emit("vector", lambda h: h.scalar_tensor_tensor(
                out=spk[:, 8:9], in0=spk[:, 1:2], scalar=-1.0,
                in1=spk[:, 7:8], op0=ALU.mult, op1=ALU.mult), inc="dv")
            ms_app = emit("scalar",
                          lambda h, p=p: h.activation(
                              out=xn_tok[p][:], in_=x_t[p][:], func=AF.Identity,
                              scale=spk[:, 7:8], bias=spk[:, 8:9]),
                          waits=[("dv", ms_nm)], inc="sc")
            xt_free[p] = [("sc", ms_app)]
            last_tr = 0
            for grp in range(2):
                u, b_, w_ = B.pt_acquire()
                for j in range(8):
                    kc = grp * 8 + j
                    waits = list(w_) if j == 0 else []
                    if grp == 0 and j == 0:
                        waits += [("sc", ms_app), ("qs", qs_consts)]
                    last_tr = emit("tensor",
                                   lambda h, p=p, kc=kc, b_=b_, j=j: h.matmul(
                                       ptp[b_][:, j * P:(j + 1) * P],
                                       xn_tok[p][:, kc * P:(kc + 1) * P], idb[:],
                                       is_transpose=True,
                                       start=(j == 0), stop=(j == 7)),
                                   waits=waits,
                                   inc="pe" if j == 7 else None)
                if i < 2:
                    dst = xn_pre3[:, grp * 8:(grp + 1) * 8, i * P:(i + 1) * P]
                else:
                    dst = xnf3[:, grp * 8:(grp + 1) * 8, (i - 2) * P:(i - 1) * P]
                ln_xnf_ms = emit("vector",
                                 lambda h, dst=dst, b_=b_: h.tensor_copy(
                                     dst,
                                     ptp[b_][:].rearrange("p (a b) -> p a b", a=8)),
                                 waits=[("pe", last_tr)], inc="dv")
                B.pt_release(u, "dv", ln_xnf_ms)
            xtk_free[p] = [("pe", last_tr)]

        for tau in range(4):
            u, b_, w_ = B.pool_acquire()
            ms = 0
            for kc in range(NKC):
                waits = list(w_) if kc == 0 else []
                if kc == 0:
                    waits += [("dv", ln_xnf_ms), ("qw", qw_consts)]
                ms = emit("tensor",
                          lambda h, kc=kc, tau=tau, b_=b_: h.matmul(
                              pool[b_][0:8, :],
                              rtb[:, kc * 8:(kc + 1) * 8],
                              xnf[:, kc * TOWN + tau * TT: kc * TOWN + (tau + 1) * TT],
                              start=(kc == 0), stop=(kc == NKC - 1)),
                          waits=waits, inc="pe" if kc == NKC - 1 else None)
            ms_ev = emit("vector",
                         lambda h, tau=tau, b_=b_: h.tensor_copy(
                             lg_fm[:, tau * TT:(tau + 1) * TT], pool[b_][0:8, :]),
                         waits=[("pe", ms)] +
                         ([("sc", B.ctr["sc"])] if tau == 0 else []), inc="dv")
            B.pool_release(u, "dv", ms_ev)
        for tc in range(NKC):
            u, b_, w_ = B.pool_acquire()
            ms = emit("tensor",
                      lambda h, tc=tc, b_=b_: h.transpose(
                          pool[b_][:, 0:8], lg_fm[0:8, tc * P:(tc + 1) * P], id8[:]),
                      waits=list(w_) + ([("dv", B.ctr["dv"]), ("qs", qs_consts)]
                                        if tc == 0 else []),
                      inc="pe")
            ms_ev = emit("vector",
                         lambda h, tc=tc, b_=b_: h.tensor_copy(
                             lt_all[:, tc * 8:(tc + 1) * 8], pool[b_][:, 0:8]),
                         waits=[("pe", ms)], inc="dv")
            B.pool_release(u, "dv", ms_ev)
        for tc in range(NKC):
            ms_mx = emit("vector", lambda h, tc=tc: h.max(
                m8_t[:], lt_all[:, tc * 8:(tc + 1) * 8]),
                waits=([("dv", B.ctr["dv"])] if tc == 0 else []), inc="dv")
            emit("vector", None, waits=[("dv", ms_mx)])
            emit("vector", lambda h, tc=tc: h.tensor_copy(
                m1_all[:, tc:tc + 1], m8_t[:, 0:1]))
            emit("vector", lambda h, tc=tc: h.tensor_copy(
                m2_all[:, tc:tc + 1], m8_t[:, 1:2]), inc="dv")
            emit("vector", lambda h, tc=tc: h.tensor_scalar(
                out=d_all[:, tc * 8:(tc + 1) * 8],
                in0=lt_all[:, tc * 8:(tc + 1) * 8],
                scalar1=m8_t[:, 1:2], scalar2=None, op0=ALU.subtract), inc="dv")
        emit("vector", None, waits=[("dv", B.ctr["dv"])])
        ms_d = emit("vector", lambda h: h.tensor_tensor(
            out=dm_all[:], in0=m1_all[:], in1=m2_all[:], op=ALU.subtract), inc="dv")
        emit("scalar", lambda h: h.activation(
            out=w1_all[:], in_=dm_all[:], func=AF.Sigmoid), waits=[("dv", ms_d)])
        emit("scalar", lambda h: h.activation(
            out=w2_all[:], in_=dm_all[:], func=AF.Sigmoid, scale=-1.0))
        ms_sg = emit("scalar", lambda h: h.activation(
            out=sgn_all[:], in_=d_all[:], func=AF.Sign), inc="sc")
        emit("vector", lambda h: h.tensor_tensor(
            out=t2_all[:], in0=sgn_all[:], in1=sgn_all[:], op=ALU.mult),
            waits=[("sc", ms_sg)])
        emit("vector", lambda h: h.tensor_scalar(
            out=buw[:], in0=w1_all[:], scalar1=0.5, scalar2=None, op0=ALU.mult))
        ms_au = emit("vector", lambda h: h.scalar_tensor_tensor(
            out=auw[:], in0=w1_all[:], scalar=0.5, in1=w2_all[:],
            op0=ALU.mult, op1=ALU.subtract), inc="dv")
        emit("vector", None, waits=[("dv", ms_au)])
        ct_ms = 0
        for tc in range(NKC):
            ct_ms = emit("vector", lambda h, tc=tc: h.tensor_scalar(
                out=ctmp_all[:, tc * 8:(tc + 1) * 8],
                in0=t2_all[:, tc * 8:(tc + 1) * 8],
                scalar1=auw[:, tc:tc + 1], scalar2=w2_all[:, tc:tc + 1],
                op0=ALU.mult, op1=ALU.add), inc="dv")
        emit("vector", None, waits=[("dv", ct_ms)])
        for tc in range(NKC):
            ct_ms = emit("vector", lambda h, tc=tc: h.scalar_tensor_tensor(
                out=ct_all[:, tc * 8:(tc + 1) * 8],
                in0=sgn_all[:, tc * 8:(tc + 1) * 8],
                scalar=buw[:, tc:tc + 1], in1=ctmp_all[:, tc * 8:(tc + 1) * 8],
                op0=ALU.mult, op1=ALU.add),
                inc="dv")
        ct_ms_f = ct_ms

        def emit_router_b():
            ms_ctb = emit("vector", lambda h: h.tensor_copy(
                ct_bf[:], ct_all[:]), waits=[("dv", ct_ms_f)], inc="dv")
            for tc in range(NKC):
                u, b_, w_ = B.pt_acquire()
                ms = emit("tensor",
                          lambda h, tc=tc, b_=b_: h.transpose(
                              ptp[b_][0:8, 0:P], ct_bf[:, tc * 8:(tc + 1) * 8],
                              idb[:]),
                          waits=list(w_) + ([("dv", ms_ctb)] if tc == 0 else []),
                          inc="pe")
                ms_ev = emit("vector",
                             lambda h, tc=tc, b_=b_: h.tensor_copy(
                                 cfb[0:8, tc * P:(tc + 1) * P], ptp[b_][0:8, 0:P]),
                             waits=[("pe", ms)], inc="dv")
                B.pt_release(u, "dv", ms_ev)

        if _DEBUG:
            emit("sync", lambda h: h.dma_start(out=ddbg_lt, in_=lt_all[:]),
                 waits=[("dv", B.ctr["dv"])], inc="qs", inc_n=16)
            emit("sync", lambda h: h.dma_start(out=ddbg_cfb, in_=cfb[0:8, :]),
                 inc="qs", inc_n=16)
        router_done = [("sc", B.ctr["sc"]), ("dv", B.ctr["dv"])]
        gu_slot_free = [[], []]
        wl_free = [[], []]
        stq_free = [[], []]
        TAUS = [(0, PRE)] + [(PRE + j * TT, TT) for j in range(4)]
        prev_cast_ms = 0
        for ec in range(NKC):
            if ec == 2:
                emit_router_b()
            p = ec % 2
            qw_ec = emit("gpsimd",
                         lambda h, ec=ec, p=p: h.dma_start(
                             out=wl[p][:],
                             in_=dwlam[ec].rearrange("p a b -> p (a b)")),
                         waits=wl_free[p], inc="qw", inc_n=16)
            sig_ms = 0
            mm_last = 0
            for (c0, n) in TAUS:
                u, b_, w_ = B.pool_acquire()
                if c0 < PRE:
                    rhs_fn = (lambda kc, c0=c0, n=n:
                              xn_pre[:, kc * PRE + c0: kc * PRE + c0 + n])
                else:
                    rhs_fn = (lambda kc, c0=c0, n=n:
                              xnf[:, kc * TOWN + (c0 - PRE):
                                  kc * TOWN + (c0 - PRE) + n])
                for kc in range(NKC):
                    waits = list(w_) if kc == 0 else []
                    if kc == 0 and c0 == 0:
                        waits += [("qw", qw_ec)]
                    mm_last = emit("tensor",
                                   lambda h, kc=kc, b_=b_, rhs_fn=rhs_fn, p=p, n=n:
                                   h.matmul(
                                       pool[b_][:, 0:n],
                                       wl[p][:, kc * P:(kc + 1) * P], rhs_fn(kc),
                                       start=(kc == 0), stop=(kc == NKC - 1)),
                                   waits=waits,
                                   inc="pe" if kc == NKC - 1 else None)
                sig_ms = emit("scalar",
                              lambda h, ec=ec, b_=b_, c0=c0, n=n: h.activation(
                                  out=lam_t[:, c0:c0 + n], in_=pool[b_][:, 0:n],
                                  func=AF.Sigmoid, bias=blamt[:, ec:ec + 1]),
                              waits=[("pe", mm_last)] +
                              ([("dv", prev_cast_ms)] if c0 == 0 else []) +
                              (router_done if ec == 0 and c0 == 0 else []),
                              inc="sc")
                B.pool_release(u, "sc", sig_ms)
            wl_free[p] = [("pe", mm_last)]
            emit("vector", lambda h, ec=ec: h.scalar_tensor_tensor(
                out=cp_t[:, 0:PRE], in0=lam_t[:, 0:PRE], scalar=-1.0,
                in1=xn_pre[:, ec * PRE:(ec + 1) * PRE], op0=ALU.add, op1=ALU.mult),
                waits=[("sc", sig_ms)])
            ms_cp = emit("vector", lambda h, ec=ec: h.scalar_tensor_tensor(
                out=cp_t[:, PRE:TLOC], in0=lam_t[:, PRE:TLOC], scalar=-1.0,
                in1=xnf[:, ec * TOWN:(ec + 1) * TOWN], op0=ALU.add, op1=ALU.mult),
                inc="dv")
            emit("vector", None, waits=[("dv", ms_cp)])
            ms_scan = emit("vector", lambda h: h.tensor_tensor_scan(
                out=s_t[:], data0=lam_t[:], data1=cp_t[:], initial=0.0,
                op0=ALU.mult, op1=ALU.subtract), inc="dv")
            emit("vector", None, waits=[("dv", ms_scan)])
            ms_cast = emit("vector", lambda h: h.tensor_copy(
                s_bf[:], s_t[:, PRE:TLOC]), waits=stq_free[0], inc="dv")
            prev_cast_ms = ms_cast
            if _DEBUG and ec == 0:
                emit("sync", lambda h: h.dma_start(out=ddbg_lam, in_=lam_t[:]),
                     waits=[("dv", ms_cast)], inc="qs", inc_n=16)
                emit("sync", lambda h: h.dma_start(out=ddbg_sbf, in_=s_bf[:]),
                     inc="qs", inc_n=16)
            ms_st = emit("scalar",
                         lambda h, ec=ec: h.dma_start(
                             out=dstok[ec * P:(ec + 1) * P, :], in_=s_bf[:]),
                         waits=[("dv", ms_cast)], inc="qc", inc_n=16)
            stq_free[0] = [("qc", ms_st)]
        qc_scan = B.ctr["qc"]

        B.barrier()
        giu = [0]
        h_free = [("pe", 0)]
        acc_init_ms = {}
        wd_gate = [B.snap()]
        silu_free = [[], []]
        su_free = []
        acc_ms_last = {}
        out_free = [[], []]
        sxf_free = [[], [], []]
        xf_free = [[], []]
        gts_free = [[], []]

        def emit_pass(bi, kind, e=None, half=None, prefetch=None,
                      ic_hook=None, down_hook=None):
            nonlocal su_free
            col0 = bi * TB
            if kind == "shared":
                wsg = lambda ic: dsgate[half * NIC + ic]
                wsu = lambda ic: dsup[half * NIC + ic]
                wsd = lambda ic: dsdown[half * NIC + ic]
                bgc = lambda ic: bgt[:, half * NIC + ic: half * NIC + ic + 1]
                buc = lambda ic: but[:, half * NIC + ic: half * NIC + ic + 1]
            else:
                wsg = lambda ic: dgate[e, ic]
                wsu = lambda ic: dup[e, ic]
                wsd = lambda ic: ddown[e, ic]

            comb_ms = None
            if kind == "expert":
                for tau in range(2):
                    u, b_, w_ = B.pool_acquire()
                    ms = emit("tensor",
                              lambda h, e=e, tau=tau, b_=b_: h.matmul(
                                  pool[b_][:],
                                  ebt[0:8, e * P:(e + 1) * P],
                                  cfb[0:8, col0 + tau * TT: col0 + (tau + 1) * TT],
                                  start=True, stop=True),
                              waits=list(w_), inc="pe")
                    comb_ms = emit("vector",
                                   lambda h, tau=tau, b_=b_: h.tensor_copy(
                                       comb_b[:, tau * TT:(tau + 1) * TT],
                                       pool[b_][:]),
                                   waits=[("pe", ms)], inc="dv")
                    B.pool_release(u, "dv", comb_ms)

            my_giu = [giu[0] + k for k in range(NIC)]
            giu[0] += NIC
            gu_qw = {}

            def emit_gu_dma(ic):
                pp = my_giu[ic] % 2
                emit("gpsimd",
                     lambda h, ic=ic, pp=pp: h.dma_start(
                         out=wg[pp][:], in_=wsg(ic).rearrange("p a b -> p (a b)")),
                     waits=list(gu_slot_free[pp]), inc="qw", inc_n=16)
                gu_qw[ic] = emit("gpsimd",
                                 lambda h, ic=ic, pp=pp: h.dma_start(
                                     out=wu[pp][:],
                                     in_=wsu(ic).rearrange("p a b -> p (a b)")),
                                 inc="qw", inc_n=16)

            if kind == "shared":
                wsd_all = dsdown[half * NIC:(half + 1) * NIC]
            else:
                wsd_all = ddown[e]

            wd_qw = [0]

            def emit_wd_all():
                wd_qw[0] = emit("gpsimd",
                                lambda h: h.dma_start(
                                    out=wd[:].rearrange("p (a m) -> p a m", a=NIC),
                                    in_=wsd_all.rearrange("a p b c -> p a (b c)")),
                                waits=wd_gate[0], inc="qw", inc_n=16)

            if prefetch is not None:
                gu_qw[0], gu_qw[1], wd_qw[0] = prefetch
            else:
                emit_gu_dma(0)
                emit_gu_dma(1)
                emit_wd_all()

            h_ready = 0
            h_ms = {}
            for ic in range(NIC):
                pp = my_giu[ic] % 2
                slot_last = 0
                si_ms = {}
                hu_ps = {}
                for mat in range(2):
                    wbuf = wg[pp] if mat == 0 else wu[pp]
                    for tau in range(2):
                        u, b_, w_ = B.pool_acquire()
                        ms = 0
                        for kc in range(NKC):
                            waits = list(w_) if kc == 0 else []
                            if kc == 0 and mat == 0 and tau == 0:
                                waits += [("qw", gu_qw[ic])]
                            ms = emit("tensor",
                                      lambda h, kc=kc, b_=b_, wbuf=wbuf, tau=tau:
                                      h.matmul(
                                          pool[b_][:],
                                          wbuf[:, kc * P:(kc + 1) * P],
                                          xnf[:, kc * TOWN + col0 + tau * TT:
                                              kc * TOWN + col0 + (tau + 1) * TT],
                                          start=(kc == 0), stop=(kc == NKC - 1)),
                                      waits=waits,
                                      inc="pe" if kc == NKC - 1 else None)
                        slot_last = ms
                        if mat == 0:
                            kw = dict(bias=bgc(ic)) if kind == "shared" else {}
                            ms_si = emit("scalar",
                                         lambda h, b_=b_, tau=tau, kw=kw:
                                         h.activation(
                                             out=silu_t[tau][:], in_=pool[b_][:],
                                             func=AF.Silu, **kw),
                                         waits=[("pe", ms)] + silu_free[tau],
                                         inc="sc")
                            B.pool_release(u, "sc", ms_si)
                            si_ms[tau] = ms_si
                        else:
                            hu_ps[tau] = (u, b_, ms)
                for tau in range(2):
                    (u, b_, ms_mm) = hu_ps[tau]
                    ms_si = si_ms[tau]
                    hdst = hbuf[:, ic * TB + tau * TT: ic * TB + (tau + 1) * TT]
                    if kind == "shared":
                        ms_su = emit("scalar",
                                     lambda h, b_=b_, ic=ic: h.activation(
                                         out=su_t[:], in_=pool[b_][:],
                                         func=AF.Identity, bias=buc(ic)),
                                     waits=[("pe", ms_mm)] + su_free, inc="sc")
                        B.pool_release(u, "sc", ms_su)
                        waits = [("sc", ms_su), ("sc", ms_si)]
                        if ic == 0 and tau == 0:
                            waits += h_free
                        ms_h = emit("vector",
                                    lambda h, hdst=hdst, tau=tau: h.tensor_tensor(
                                        out=hdst, in0=silu_t[tau][:], in1=su_t[:],
                                        op=ALU.mult),
                                    waits=waits, inc="dv")
                        su_free = [("dv", ms_h)]
                    else:
                        waits = [("sc", ms_si), ("pe", ms_mm)] + su_free
                        if ic == 0 and tau == 0:
                            waits += h_free + [("dv", comb_ms)]
                        ms_m = emit("vector",
                                    lambda h, b_=b_, tau=tau: h.tensor_tensor(
                                        out=su_t[:], in0=silu_t[tau][:],
                                        in1=pool[b_][:], op=ALU.mult),
                                    waits=waits, inc="dv")
                        B.pool_release(u, "dv", ms_m)
                        ms_h = emit("vector",
                                    lambda h, hdst=hdst, tau=tau: h.tensor_tensor(
                                        out=hdst, in0=su_t[:],
                                        in1=comb_b[:, tau * TT:(tau + 1) * TT],
                                        op=ALU.mult),
                                    waits=[("dv", ms_m)], inc="dv")
                        su_free = [("dv", ms_h)]
                    silu_free[tau] = [("dv", ms_h)]
                    h_ready = ms_h
                    h_ms[ic] = ms_h
                    if (_DEBUG and bi == 0 and kind == "shared" and half == 0
                            and ic == 0 and tau == 0):
                        emit("sync", lambda h, pp=pp: h.dma_start(
                            out=ddbg_wg, in_=wg[pp][:]),
                            waits=[("dv", ms_h)], inc="qs", inc_n=16)
                        emit("sync", lambda h: h.dma_start(
                            out=ddbg_sil, in_=silu_t[0][:]), inc="qs", inc_n=16)
                        emit("sync", lambda h: h.dma_start(
                            out=ddbg_su, in_=su_t[:]), inc="qs", inc_n=16)
                        B.barrier()
                gu_slot_free[pp] = [("pe", slot_last)]
                if ic + 2 < NIC:
                    emit_gu_dma(ic + 2)
                if ic_hook is not None:
                    ic_hook(ic)

            down_last = 0
            ms_ac = 0
            for dc in range(NDC):
                for tau in range(2):
                    u, b_, w_ = B.pool_acquire()
                    ms = 0
                    for j in range(NIC):
                        waits = list(w_) if j == 0 else []
                        if dc == 0 and tau == 0:
                            waits += [("dv", h_ms[j])]
                            if j == 0:
                                waits += [("qw", wd_qw[0])]
                        ms = emit("tensor",
                                  lambda h, j=j, dc=dc, tau=tau, b_=b_: h.matmul(
                                      pool[b_][:],
                                      wd[:, j * D + dc * P: j * D + (dc + 1) * P],
                                      hbuf[:, j * TB + tau * TT:
                                           j * TB + (tau + 1) * TT],
                                      start=(j == 0), stop=(j == NIC - 1)),
                                  waits=waits,
                                  inc="pe" if j == NIC - 1 else None)
                    down_last = ms
                    aslice = accb[:, dc * TB + tau * TT: dc * TB + (tau + 1) * TT]
                    if kind == "shared" and half == 0:
                        ms_ac = emit("scalar",
                                     lambda h, b_=b_, aslice=aslice, dc=dc:
                                     h.activation(
                                         out=aslice, in_=pool[b_][:],
                                         func=AF.Identity, bias=bdt[:, dc:dc + 1]),
                                     waits=[("pe", ms)], inc="sc")
                        acc_init_ms[(dc, tau)] = ms_ac
                        B.pool_release(u, "sc", ms_ac)
                    else:
                        waits = [("pe", ms)]
                        if kind == "shared" and half == 1:
                            waits += [("sc", acc_init_ms[(dc, tau)])]
                        ms_ac = emit("vector",
                                     lambda h, b_=b_, aslice=aslice: h.tensor_tensor(
                                         out=aslice, in0=aslice, in1=pool[b_][:],
                                         op=ALU.add),
                                     waits=waits, inc="dv")
                        B.pool_release(u, "dv", ms_ac)
                    acc_ms_last[(dc, tau)] = ms_ac
                if down_hook is not None:
                    down_hook(dc)
            h_free[:] = [("pe", down_last)]
            wd_gate[0] = [("pe", down_last)]

        acc3 = accb.rearrange("p (k t) -> p k t", k=NDC)

        fin_seq = [0]

        def emit_final_piece(bi, tc, ds, acc_dep):
            tca = bi * (TB // P) + tc
            pc = fin_seq[0]
            fin_seq[0] += 1
            sp = pc % 2
            s4 = pc % 3
            gp = pc % 2
            ms_ld_s = emit("sync",
                           lambda h, tca=tca, ds=ds, s4=s4:
                           h.dma_start_transpose(
                               stokt[s4][:],
                               dstok[ds * TT:(ds + 1) * TT,
                                     tca * P:(tca + 1) * P]),
                           waits=list(sxf_free[s4]) +
                           ([("qc", qc_scan)] if fin_seq[0] == 1 else []),
                           inc="qs", inc_n=16)
            ms_ld_x = emit("sync",
                           lambda h, tca=tca, ds=ds, sp=sp: h.dma_start(
                               out=xft[sp][:],
                               in_=dx[PRE + tca * P: PRE + (tca + 1) * P,
                                      ds * TT:(ds + 1) * TT]),
                           waits=xf_free[sp], inc="qs", inc_n=16)
            u, b_, w_ = B.pt_acquire()
            ms_tr = 0
            for j in range(4):
                dc = ds * 4 + j
                ms_tr = emit("tensor",
                             lambda h, b_=b_, j=j, dc=dc, tc=tc: h.matmul(
                                 ptp[b_][:, j * P:(j + 1) * P],
                                 acc3[:, dc, tc * P:(tc + 1) * P],
                                 idb[:], is_transpose=True,
                                 start=(j == 0), stop=(j == 3)),
                             waits=(list(w_) + list(acc_dep)) if j == 0 else [],
                             inc="pe" if j == 3 else None)
            ms_sil = emit("scalar",
                          lambda h, gp=gp, b_=b_: h.activation(
                              out=gts_t[gp][:], in_=ptp[b_][:, 0:TT],
                              func=AF.Silu),
                          waits=[("pe", ms_tr)] + gts_free[gp], inc="sc")
            B.pt_release(u, "sc", ms_sil)
            ms_y = emit("vector",
                        lambda h, sp=sp, s4=s4, gp=gp: h.tensor_tensor(
                            out=ytmp[:], in0=stokt[s4][:],
                            in1=gts_t[gp][:], op=ALU.mult),
                        waits=[("qs", ms_ld_s), ("sc", ms_sil)], inc="dv")
            sxf_free[s4] = [("dv", ms_y)]
            gts_free[gp] = [("dv", ms_y)]
            ms_o = emit("vector",
                        lambda h, sp=sp: h.tensor_tensor(
                            out=outt[sp][:], in0=ytmp[:], in1=xft[sp][:],
                            op=ALU.add),
                        waits=[("qs", ms_ld_x), ("dv", ms_y)] + out_free[sp],
                        inc="dv")
            xf_free[sp] = [("dv", ms_o)]
            ms_st = emit("scalar",
                         lambda h, tca=tca, ds=ds, sp=sp: h.dma_start(
                             out=dout[tca * P:(tca + 1) * P,
                                      ds * TT:(ds + 1) * TT],
                             in_=outt[sp][:]),
                         waits=[("dv", ms_o)], inc="qc", inc_n=16)
            out_free[sp] = [("qc", ms_st)]

        def make_ic_hook(bi_prev, acc_done):
            state = [0]

            def hook(ic):
                if ic >= 8:
                    return
                for _ in range(8):
                    if state[0] < 4 * (TB // P):
                        emit_final_piece(bi_prev, state[0] // 4, state[0] % 4,
                                         [("dv", acc_done)])
                        state[0] += 1
            return hook

        acc_done_v = [0] * NBLK
        for bi in range(NBLK):
            hook = (make_ic_hook(bi - 1, acc_done_v[bi - 1]) if bi > 0 else None)
            emit_pass(bi, "shared", half=0,
                      prefetch=(prefetch0 if bi == 0 else None),
                      ic_hook=hook)
            if _DEBUG and bi == 0:
                emit("sync", lambda h: h.dma_start(out=ddbg_h, in_=hbuf[:, 0:TB]),
                     waits=[("dv", B.ctr["dv"]), ("pe", B.ctr["pe"])],
                     inc="qs", inc_n=16)
                B.barrier()
            emit_pass(bi, "shared", half=1)
            dh = None
            if bi == NBLK - 1:
                fin_state = [0]

                def dh(dc, fin_state=fin_state):
                    avail = (dc - 3) // 4
                    if avail < 0:
                        return
                    budget = 64 if dc == NDC - 1 else 6
                    while budget > 0 and fin_state[0] < 4 * (TB // P):
                        pc = fin_state[0]
                        ds, tc = pc // (TB // P), pc % (TB // P)
                        if ds > avail:
                            break
                        dep = [("dv", max(acc_ms_last[(c, t)]
                                          for c in range(ds * 4, ds * 4 + 4)
                                          for t in range(2)))]
                        emit_final_piece(NBLK - 1, tc, ds, dep)
                        fin_state[0] += 1
                        budget -= 1
            for e in range(E):
                emit_pass(bi, "expert", e=e,
                          down_hook=(dh if e == E - 1 else None))
            acc_done_v[bi] = B.ctr["dv"]
            if _DEBUG and bi == 0:
                emit("sync", lambda h: h.dma_start(out=ddbg_acc, in_=accb[:]),
                     waits=[("dv", acc_done_v[0])], inc="qs", inc_n=16)
                B.barrier()

        emit("sync", None, waits=[("qs", B.ctr["qs"]), ("qc", B.ctr["qc"])])
        emit("scalar", None, waits=[("qc", B.ctr["qc"])])

        def make_runner(entries):
            def run(h):
                for (w, build, inc, inc_n) in entries:
                    for (sname, v) in w:
                        h.wait_ge(sem[sname], v)
                    if build is not None:
                        ins = build(h)
                        if inc is not None:
                            ins.then_inc(sem[inc], inc_n)
            return run

        blk.gpsimd(make_runner(B.script["gpsimd"]))
        blk.sync(make_runner(B.script["sync"]))
        blk.scalar(make_runner(B.script["scalar"]))
        blk.vector(make_runner(B.script["vector"]))
        blk.tensor(make_runner(B.script["tensor"]))

    return nc


_NC_CACHE = []


def kernel(**inputs):
    maps = host_prep(inputs)
    if not _NC_CACHE:
        _NC_CACHE.append(build_program())
    nc = _NC_CACHE[0]
    res = run_bass_kernel_spmd(nc, maps, core_ids=list(range(8)))
    out = np.concatenate([res.results[c]["out"] for c in range(8)], axis=0)
    return np.ascontiguousarray(out.reshape(4, 4096, 2048))
